# version 01
# baseline (speedup 1.0000x reference)
import contextlib
import math

import numpy as np
import ml_dtypes

import concourse.bass as bass
import concourse.mybir as mybir
from concourse.bass_utils import run_bass_kernel_spmd

F32 = mybir.dt.float32
BF16 = mybir.dt.bfloat16
AF = mybir.ActivationFunctionType
OP = mybir.AluOpType

NCORE = 8
B = 2
S = 8192
D = 2048
TOK = 2048
NBLK = 4
BLK = 512
NT = 16
EPS = 1e-5

BLOCKS = {g: sorted([g, 7 - g, 8 + g, 15 - g]) for g in range(4)}

ENGINES = ("tensor", "vector", "scalar", "gpsimd", "sync")
DMA_QUEUES = ("sync", "scalar", "gpsimd")



def _conflict(k1, k2):
    return k1 is None or k2 is None or k1 == k2


class _Op:
    __slots__ = ("idx", "engine", "domain", "fn", "reads", "writes", "amount",
                 "inc_idx", "deps", "barrier")

    def __init__(self, idx, engine, domain, fn, reads, writes, amount):
        self.idx = idx
        self.engine = engine
        self.domain = domain
        self.fn = fn
        self.reads = reads
        self.writes = writes
        self.amount = amount
        self.inc_idx = None
        self.deps = []
        self.barrier = False


class Prog:

    def __init__(self, nc: bass.Bass):
        self.nc = nc
        self.ops = []
        self._stack = [contextlib.ExitStack()]
        self._names = {}
        self._const_cache = {}
        self._ident_bf = None
        self._ident_f32 = None
        self._ones_row = None

    def _uname(self, name):
        n = self._names.get(name, 0)
        self._names[name] = n + 1
        return f"{name}_{n}" if n else name

    def sb(self, name, shape, dtype):
        g = self.nc.sbuf_tensor(self._uname(name), shape, dtype)
        return self._stack[-1].enter_context(g)

    def ps(self, name, shape, dtype=F32):
        g = self.nc.psum_tensor(self._uname(name), shape, dtype)
        return self._stack[-1].enter_context(g)

    @contextlib.contextmanager
    def scope(self):
        es = contextlib.ExitStack()
        self._stack.append(es)
        try:
            yield
        finally:
            self.barrier()
            self._stack.pop()
            es.close()

    def op(self, engine, fn, reads=(), writes=(), dma=False, amount=None):
        if dma:
            domain = ("dma", engine)
            amt = 16
        else:
            domain = ("eng", engine)
            amt = 1
        if amount is not None:
            amt = amount
        o = _Op(len(self.ops), engine, domain, fn, list(reads), list(writes), amt)
        self.ops.append(o)
        return o

    def collective(self, fn, reads=(), writes=()):
        o = _Op(len(self.ops), "gpsimd", ("cc",), fn, list(reads), list(writes), 1)
        self.ops.append(o)
        return o

    def barrier(self):
        o = _Op(len(self.ops), None, None, None, [], [], 0)
        o.barrier = True
        self.ops.append(o)
        return o

    def ident_bf(self):
        if self._ident_bf is None:
            dram = self.nc.inline_tensor(np.eye(128, dtype=ml_dtypes.bfloat16), self._uname("identbf"))
            t = self.sb("identbf_sb", [128, 128], BF16)
            self.op("sync", lambda e, t=t, d=dram: e.dma_start(t[:], d.ap()),
                    reads=[], writes=[(id(t), None)], dma=True)
            self._ident_bf = t
        return self._ident_bf

    def ident_f32(self):
        if self._ident_f32 is None:
            dram = self.nc.inline_tensor(np.eye(128, dtype=np.float32), self._uname("identf32"))
            t = self.sb("identf32_sb", [128, 128], F32)
            self.op("sync", lambda e, t=t, d=dram: e.dma_start(t[:], d.ap()),
                    reads=[], writes=[(id(t), None)], dma=True)
            self._ident_f32 = t
        return self._ident_f32

    def ones_row(self):
        if self._ones_row is None:
            t = self.sb("ones_row", [1, 128], F32)
            self.op("vector", lambda e, t=t: e.memset(t[:], 1.0),
                    reads=[], writes=[(id(t), None)])
            self._ones_row = t
        return self._ones_row

    def const_col(self, value):
        if value not in self._const_cache:
            t = self.sb("constcol", [128, 1], F32)
            self.op("vector", lambda e, t=t, v=value: e.memset(t[:], v),
                    reads=[], writes=[(id(t), None)])
            self._const_cache[value] = t
        return self._const_cache[value]

    DMA_POOL = 10

    def finalize(self):
        nc = self.nc
        writers = {}
        readers = {}
        all_domains = []
        dom_ops = {}

        def add_dep(o, j):
            if j == o.idx:
                return
            o.deps.append(j)

        for o in self.ops:
            if o.barrier:
                continue
            if o.domain not in dom_ops:
                dom_ops[o.domain] = []
                all_domains.append(o.domain)
            dom_ops[o.domain].append(o.idx)
            for (t, k) in o.reads:
                for (j, wk) in writers.get(t, ()):
                    if _conflict(k, wk):
                        add_dep(o, j)
            for (t, k) in o.writes:
                for (j, wk) in writers.get(t, ()):
                    if _conflict(k, wk):
                        add_dep(o, j)
                for (j, rk) in readers.get(t, ()):
                    if _conflict(k, rk):
                        add_dep(o, j)
            for (t, k) in o.reads:
                readers.setdefault(t, []).append((o.idx, k))
            for (t, k) in o.writes:
                wl = writers.setdefault(t, [])
                wl[:] = [(j, wk) for (j, wk) in wl if not _covers(k, wk)]
                wl.append((o.idx, k))
                rl = readers.setdefault(t, [])
                rl[:] = [(j, rk) for (j, rk) in rl if not _covers(k, rk)]

        opl = self.ops
        awaited = set()
        for o in opl:
            if o.barrier:
                continue
            me = o.domain
            for j in o.deps:
                jo = opl[j]
                if jo.domain == me and me == ("eng", "tensor"):
                    continue
                awaited.add(j)
        last_at_barrier = []
        cur_last = {}
        for o in opl:
            if o.barrier:
                last_at_barrier.append(dict(cur_last))
                for j in cur_last.values():
                    awaited.add(j)
            else:
                cur_last[o.domain] = o.idx

        dom_count = {}
        pool = {}
        pool_n = {}
        op_sem = {}
        op_issue_wait = {}
        sems = {}

        def dom_sem(d):
            if d not in sems:
                nm = "sem_" + "_".join(str(x) for x in d)
                sems[d] = nc.alloc_semaphore(self._uname(nm))
            return sems[d]

        for o in opl:
            if o.barrier:
                continue
            if o.domain[0] in ("dma", "cc"):
                if o.domain not in pool:
                    np_ = self.DMA_POOL
                    pool[o.domain] = [
                        nc.alloc_semaphore(self._uname(
                            "pool_" + "_".join(str(x) for x in o.domain)))
                        for _ in range(np_)]
                    pool_n[o.domain] = 0
                n = pool_n[o.domain]
                pool_n[o.domain] = n + 1
                P = len(pool[o.domain])
                s = pool[o.domain][n % P]
                use = n // P + 1
                o.inc_idx = use
                op_sem[o.idx] = (s, use * o.amount)
                if n >= P:
                    op_issue_wait[o.idx] = (s, (use - 1) * o.amount)
            elif o.idx in awaited:
                d = o.domain
                dom_count[d] = dom_count.get(d, 0) + o.amount
                o.inc_idx = dom_count[d]
                op_sem[o.idx] = (dom_sem(d), o.inc_idx)

        known = {e: {} for e in ENGINES}

        emit_waits = {}
        nbar = 0
        for o in opl:
            if o.barrier:
                lasts = last_at_barrier[nbar]
                nbar += 1
                w = {}
                for d, j in lasts.items():
                    if j in op_sem:
                        s, v = op_sem[j]
                        w[s] = max(w.get(s, 0), v)
                for d, pl in pool.items():
                    npl = pool_n[d]
                    pass
                wl = sorted(w.items(), key=lambda x: x[0].name)
                emit_waits[o.idx] = wl
                for e in ENGINES:
                    for s, v in wl:
                        known[e][s] = max(known[e].get(s, 0), v)
                continue
            w = {}
            kn = known[o.engine]
            for j in sorted(set(o.deps)):
                jo = opl[j]
                if jo.domain == o.domain and o.domain == ("eng", "tensor"):
                    continue
                assert j in op_sem, f"op {j} has no completion sem but is a dep"
                s, v = op_sem[j]
                if kn.get(s, 0) >= v:
                    continue
                w[s] = max(w.get(s, 0), v)
            if o.idx in op_issue_wait:
                s, v = op_issue_wait[o.idx]
                if v > 0 and kn.get(s, 0) < v:
                    w[s] = max(w.get(s, 0), v)
            wl = sorted(w.items(), key=lambda x: x[0].name)
            emit_waits[o.idx] = wl
            for s, v in wl:
                kn[s] = max(kn.get(s, 0), v)

        slot_state = {}
        nbar = 0
        bar_waits = {}
        for o in opl:
            if o.barrier:
                wl = sorted(slot_state.items(), key=lambda x: x[0].name)
                bar_waits[o.idx] = wl
                continue
            if o.idx in op_sem:
                s, v = op_sem[o.idx]
                slot_state[s] = max(slot_state.get(s, 0), v)
        for o in opl:
            if o.barrier:
                emit_waits[o.idx] = bar_waits[o.idx]

        used_engines = [e for e in ENGINES
                        if any((not o.barrier) and o.engine == e for o in opl)]
        eng_ops = {e: [o for o in opl if o.barrier or o.engine == e] for e in used_engines}

        with nc.Block() as blk:
            for e in used_engines:
                def make_body(e):
                    def body(eng):
                        for o in eng_ops[e]:
                            for s, v in emit_waits[o.idx]:
                                eng.wait_ge(s, v)
                            if o.barrier:
                                continue
                            ins = o.fn(eng)
                            if o.idx in op_sem:
                                s, v = op_sem[o.idx]
                                ins.then_inc(s, o.amount)
                    return body
                getattr(blk, e)(make_body(e))
        self._stack[0].close()


def _covers(knew, kold):
    return knew is None or knew == kold


def R(t, key=None):
    return (id(t), key)


def W(t, key=None):
    return (id(t), key)



def emit_ln_tile(pg, xt, xn_out, stats, aggr, rstd, eps_col, out_dtype_note=None):
    for i in range(4):
        pg.op("vector", lambda e, i=i, xt=xt, stats=stats:
              e.bn_stats(stats[:, 6 * i:6 * i + 6], xt[:, 512 * i:512 * (i + 1)]),
              reads=[R(xt)], writes=[W(stats, i)])
    pg.op("vector", lambda e, stats=stats, aggr=aggr: e.bn_aggr(aggr[:], stats[:]),
          reads=[R(stats)], writes=[W(aggr)])
    pg.op("scalar", lambda e, aggr=aggr, rstd=rstd, ec=eps_col:
          e.activation(rstd[:], aggr[:, 1:2], AF.Ln, bias=ec[:]),
          reads=[R(aggr), R(eps_col)], writes=[W(rstd)])
    pg.op("scalar", lambda e, rstd=rstd: e.activation(rstd[:], rstd[:], AF.Exp, scale=-0.5),
          reads=[R(rstd)], writes=[W(rstd)])
    pg.op("vector", lambda e, xt=xt, xn=xn_out, aggr=aggr, rstd=rstd:
          e.tensor_scalar(xn[:], xt[:], aggr[:, 0:1], rstd[:], OP.subtract, OP.mult),
          reads=[R(xt), R(aggr), R(rstd)], writes=[W(xn_out)])


def emit_transpose_to(pg, src_bf, src_col0, dst_bf, dst_col0, psT, ident):
    pg.op("tensor", lambda e, s=src_bf, c=src_col0, p=psT, i=ident:
          e.transpose(p[:], s[:, c:c + 128], i[:]),
          reads=[R(src_bf), R(ident)], writes=[W(psT)])
    pg.op("vector", lambda e, p=psT, d=dst_bf, c=dst_col0:
          e.tensor_copy(d[:, c:c + 128], p[:]),
          reads=[R(psT)], writes=[W(dst_bf, dst_col0)])


def emit_ffn0(pg: Prog, x_ap, factors_ap, bias_ap, xo_ap):
    nc = pg.nc
    c0 = 0.5 / math.sqrt(8.0)
    with pg.scope():
        ident = pg.ident_bf()
        ones = pg.ones_row()
        eps_col = pg.const_col(EPS)

        F_bf = pg.sb("Fbf", [128, 16 * 2048], BF16)
        stg = [pg.sb(f"Fstg{i}", [128, 2048], F32) for i in range(2)]
        bias_row = pg.sb("biasrow", [1, 2048], F32)
        bias_half = pg.sb("biashalf", [128, 2048], F32)
        ps_bb = pg.ps("ps_bb", [128, 512], F32)
        xt = [pg.sb(f"xt{i}", [128, 2048], F32) for i in range(4)]
        xn = [pg.sb(f"xn{i}", [128, 2048], BF16) for i in range(4)]
        xnF = pg.sb("xnF", [128, 16 * 512], BF16)
        stats = pg.sb("stats", [128, 24], F32)
        aggr = [pg.sb(f"aggr{i}", [128, 2], F32) for i in range(4)]
        rstd = [pg.sb(f"rstd{i}", [128, 1], F32) for i in range(4)]
        psT = [pg.ps(f"psT{i}", [128, 128], BF16) for i in range(2)]
        ps_out = [pg.ps(f"psO{i}", [128, 512], F32) for i in range(4)]
        tmp = [pg.sb(f"tmp{i}", [128, 512], F32) for i in range(2)]

        pg.op("sync", lambda e: e.dma_start(bias_row[:], bias_ap),
              reads=[], writes=[W(bias_row)], dma=True)
        for oc in range(4):
            pg.op("tensor", lambda e, oc=oc: e.matmul(
                ps_bb[:], ones[:], bias_row[:, oc * 512:(oc + 1) * 512],
                start=True, stop=True),
                reads=[R(ones), R(bias_row)], writes=[W(ps_bb)])
            pg.op("scalar", lambda e, oc=oc: e.activation(
                bias_half[:, oc * 512:(oc + 1) * 512], ps_bb[:], AF.Copy, scale=0.5),
                reads=[R(ps_bb)], writes=[W(bias_half, oc)])

        for kc in range(16):
            s = stg[kc % 2]
            pg.op("gpsimd", lambda e, kc=kc, s=s: e.dma_start(
                s[:], factors_ap[0, kc * 128:(kc + 1) * 128, :]),
                reads=[], writes=[W(s)], dma=True)
            for r in range(1, 8):
                pg.op("gpsimd", lambda e, kc=kc, r=r, s=s: e.dma_start(
                    s[:], factors_ap[r, kc * 128:(kc + 1) * 128, :], accum_op=OP.add),
                    reads=[R(s)], writes=[W(s)], dma=True)
            pg.op("vector", lambda e, kc=kc, s=s: e.tensor_copy(
                F_bf[:, kc * 2048:(kc + 1) * 2048], s[:]),
                reads=[R(s)], writes=[W(F_bf, kc)])

        for blk in range(4):
            for t in range(4):
                gt = blk * 4 + t
                xtt = xt[t]
                pg.op("sync", lambda e, gt=gt, xtt=xtt: e.dma_start(
                    xtt[:], x_ap[gt * 128:(gt + 1) * 128, :]),
                    reads=[], writes=[W(xtt)], dma=True)
                emit_ln_tile(pg, xt[t], xn[t], stats, aggr[t], rstd[t], eps_col)
                for kc in range(16):
                    emit_transpose_to(pg, xn[t], kc * 128, xnF, kc * 512 + t * 128,
                                      psT[(t * 16 + kc) % 2], ident)
            for t in range(4):
                for kc in range(16):
                    for oc in range(4):
                        pg.op("tensor", lambda e, t=t, kc=kc, oc=oc: e.matmul(
                            ps_out[oc][:],
                            xnF[:, kc * 512 + t * 128: kc * 512 + (t + 1) * 128],
                            F_bf[:, kc * 2048 + oc * 512: kc * 2048 + (oc + 1) * 512],
                            start=(kc == 0), stop=(kc == 15)),
                            reads=[R(xnF), R(F_bf)], writes=[W(ps_out[oc])])
                for oc in range(4):
                    tm = tmp[oc % 2]
                    pg.op("vector", lambda e, oc=oc, tm=tm: e.scalar_tensor_tensor(
                        tm[:], ps_out[oc][:], c0, bias_half[:, oc * 512:(oc + 1) * 512],
                        OP.mult, OP.add),
                        reads=[R(ps_out[oc]), R(bias_half, oc)], writes=[W(tm)])
                    pg.op("vector", lambda e, oc=oc, tm=tm, xt=xt[t]: e.tensor_tensor(
                        xt[:, oc * 512:(oc + 1) * 512], xt[:, oc * 512:(oc + 1) * 512],
                        tm[:], OP.add),
                        reads=[R(tmp[oc % 2]), R(xt[t], oc)], writes=[W(xt[t], oc)])
                gt = blk * 4 + t
                xtt = xt[t]
                pg.op("sync", lambda e, gt=gt, xtt=xtt: e.dma_start(
                    xo_ap[gt * 128:(gt + 1) * 128, :], xtt[:]),
                    reads=[R(xtt)], writes=[], dma=True)


def emit_ln_all(pg, x_ap, xnF, xt, xn, stats, aggr, rstd, eps_col, psT, ident,
                ncols=2048, tiles=range(16), xn32=None):
    tiles = list(tiles)
    for i, gt in enumerate(tiles):
        xtt = xt[i % len(xt)]
        xnn = xn[i % len(xn)]
        pg.op("sync", lambda e, gt=gt, xtt=xtt: e.dma_start(
            xtt[:], x_ap[gt * 128:(gt + 1) * 128, :]),
            reads=[], writes=[W(xtt)], dma=True)
        emit_ln_tile(pg, xtt, xnn, stats, aggr[i % len(aggr)], rstd[i % len(rstd)], eps_col)
        for kc in range(16):
            emit_transpose_to(pg, xnn, kc * 128, xnF, kc * ncols + i * 128,
                              psT[(i * 16 + kc) % len(psT)], ident)


def emit_wcast_load(pg, w_ap, dst, row0, col0, nrows_chunks, ncols, queue="gpsimd"):
    for kc in range(nrows_chunks):
        pg.op(queue, lambda e, kc=kc: e.dma_start(
            dst[:, kc * ncols:(kc + 1) * ncols],
            w_ap[row0 + kc * 128: row0 + (kc + 1) * 128, col0:col0 + ncols]),
            reads=[], writes=[W(dst, kc)], dma=True)


def emit_proj_fm(pg, xnF, w_bf, out_cb, ps_out, stage, nton=4, nk=16, noc=16,
                 bias_cols=None, w_keyed=True):
    ncols = nton * 512
    for oc in range(noc):
        for tc in range(nton):
            ps = ps_out[(oc * nton + tc) % len(ps_out)]
            for k in range(nk):
                pg.op("tensor", lambda e, oc=oc, tc=tc, k=k, ps=ps: e.matmul(
                    ps[:],
                    w_bf[:, k * 2048 + oc * 128: k * 2048 + (oc + 1) * 128],
                    xnF[:, k * ncols + tc * 512: k * ncols + (tc + 1) * 512],
                    start=(k == 0), stop=(k == nk - 1)),
                    reads=[R(w_bf, k) if w_keyed else R(w_bf), R(xnF)],
                    writes=[W(ps)])
            out_cb(oc, tc, ps, stage[(oc * nton + tc) % len(stage)])


def emit_l0_proj(pg: Prog, x_ap, wq_ap, wk_ap, wv_ap, wqi_ap, wki_ap,
                 q_fm_ap, k_fm_ap, v_tm_ap, qp_fm_ap, kp_fm_ap):
    nc = pg.nc
    with pg.scope():
        ident = pg.ident_bf()
        eps_col = pg.const_col(EPS)
        xnF = pg.sb("xnF_all", [128, 16 * 2048], BF16)
        with pg.scope():
            xt = [pg.sb(f"xt{i}", [128, 2048], F32) for i in range(2)]
            xn = [pg.sb(f"xn{i}", [128, 2048], BF16) for i in range(2)]
            stats = pg.sb("stats", [128, 24], F32)
            aggr = [pg.sb(f"aggr{i}", [128, 2], F32) for i in range(2)]
            rstd = [pg.sb(f"rstd{i}", [128, 1], F32) for i in range(2)]
            psT = [pg.ps(f"psT{i}", [128, 128], BF16) for i in range(2)]
            emit_ln_all(pg, x_ap, xnF, xt, xn, stats, aggr, rstd, eps_col, psT, ident)

        w_bf = pg.sb("w_bf", [128, 16 * 2048], BF16)
        lin_bf = pg.sb("lin_bf", [128, 16 * 2048], BF16)
        ps_out = [pg.ps(f"psP{i}", [128, 512], F32) for i in range(4)]
        stage = [pg.sb(f"stgP{i}", [128, 512], BF16) for i in range(4)]

        def store_fm(dst_ap):
            def cb(oc, tc, ps, stg):
                pg.op("scalar", lambda e, ps=ps, stg=stg: e.activation(
                    stg[:], ps[:], AF.Copy),
                    reads=[R(ps)], writes=[W(stg)])
                pg.op("sync", lambda e, oc=oc, tc=tc, stg=stg: e.dma_start(
                    dst_ap[oc * 128:(oc + 1) * 128, tc * 512:(tc + 1) * 512], stg[:]),
                    reads=[R(stg)], writes=[], dma=True)
            return cb

        emit_wcast_load(pg, wq_ap, w_bf, 0, 0, 16, 2048)
        emit_proj_fm(pg, xnF, w_bf, store_fm(q_fm_ap), ps_out, stage)
        emit_wcast_load(pg, wk_ap, w_bf, 0, 0, 16, 2048)
        emit_proj_fm(pg, xnF, w_bf, store_fm(k_fm_ap), ps_out, stage)
        emit_wcast_load(pg, wv_ap, w_bf, 0, 0, 16, 2048)
        vstage = pg.sb("vstage", [128, 2048], BF16)
        for lt in range(16):
            for oc in range(4):
                ps = ps_out[oc]
                for k in range(16):
                    pg.op("tensor", lambda e, lt=lt, oc=oc, k=k, ps=ps: e.matmul(
                        ps[:],
                        xnF[:, k * 2048 + lt * 128: k * 2048 + (lt + 1) * 128],
                        w_bf[:, k * 2048 + oc * 512: k * 2048 + (oc + 1) * 512],
                        start=(k == 0), stop=(k == 15)),
                        reads=[R(w_bf, k), R(xnF)], writes=[W(ps)])
                pg.op("scalar", lambda e, oc=oc, ps=ps: e.activation(
                    vstage[:, oc * 512:(oc + 1) * 512], ps[:], AF.Copy),
                    reads=[R(ps)], writes=[W(vstage, oc)])
            pg.op("sync", lambda e, lt=lt: e.dma_start(
                v_tm_ap[lt * 128:(lt + 1) * 128, :], vstage[:]),
                reads=[R(vstage)], writes=[], dma=True)
        for kc in range(16):
            pg.op("scalar", lambda e, kc=kc: e.dma_start(
                lin_bf[:, kc * 2048:(kc + 1) * 2048],
                q_fm_ap[kc * 128:(kc + 1) * 128, :]),
                reads=[], writes=[W(lin_bf, kc)], dma=True)
        emit_wcast_load(pg, wqi_ap, w_bf, 0, 0, 16, 2048)
        emit_proj_fm(pg, lin_bf, w_bf, store_fm(qp_fm_ap), ps_out, stage)
        for kc in range(16):
            pg.op("scalar", lambda e, kc=kc: e.dma_start(
                lin_bf[:, kc * 2048:(kc + 1) * 2048],
                k_fm_ap[kc * 128:(kc + 1) * 128, :]),
                reads=[], writes=[W(lin_bf, kc)], dma=True)
        emit_wcast_load(pg, wki_ap, w_bf, 0, 0, 16, 2048)
        emit_proj_fm(pg, lin_bf, w_bf, store_fm(kp_fm_ap), ps_out, stage)


def emit_l0_rel(pg: Prog, qp_fm_ap, kp_all_ap, hw_ap, temp_ap, rel_ap):
    with pg.scope():
        ones = pg.ones_row()
        qp_h = [pg.sb(f"qp_h{i}", [128, 2 * 2048], BF16) for i in range(2)]
        kp_h = [pg.sb(f"kp_h{i}", [128, 2 * 8192], BF16) for i in range(2)]
        junk = [pg.sb(f"junk{i}", [128, 512], BF16) for i in range(2)]
        sumcols = pg.sb("sumcols", [128, 16], F32)
        headsum = pg.sb("headsum", [128, 1], F32)
        rel_acc = pg.sb("rel_acc", [128, 16], F32)
        w_row = pg.sb("w_row", [1, 8], F32)
        t_row = pg.sb("t_row", [1, 1], F32)
        w_bc = pg.sb("w_bc", [128, 8], F32)
        t_bc = pg.sb("t_bc", [128, 1], F32)
        ps_s = [pg.ps(f"psS{i}", [128, 512], F32) for i in range(2)]
        ps_b = pg.ps("psB", [128, 8], F32)

        pg.op("sync", lambda e: e.dma_start(w_row[:], hw_ap),
              reads=[], writes=[W(w_row)], dma=True)
        pg.op("sync", lambda e: e.dma_start(t_row[:], temp_ap),
              reads=[], writes=[W(t_row)], dma=True)
        pg.op("scalar", lambda e: e.activation(t_row[:], t_row[:], AF.Exp, scale=-1.0),
              reads=[R(t_row)], writes=[W(t_row)])
        pg.op("tensor", lambda e: e.matmul(ps_b[:], ones[:], w_row[:], start=True, stop=True),
              reads=[R(ones), R(w_row)], writes=[W(ps_b)])
        pg.op("vector", lambda e: e.tensor_copy(w_bc[:], ps_b[:]),
              reads=[R(ps_b)], writes=[W(w_bc)])
        pg.op("tensor", lambda e: e.matmul(ps_b[:, 0:1], ones[:], t_row[:], start=True, stop=True),
              reads=[R(ones), R(t_row)], writes=[W(ps_b)])
        pg.op("vector", lambda e: e.tensor_copy(t_bc[:], ps_b[:, 0:1]),
              reads=[R(ps_b)], writes=[W(t_bc)])
        pg.op("vector", lambda e: e.memset(rel_acc[:], 0.0),
              reads=[], writes=[W(rel_acc)])

        for h in range(8):
            qph = qp_h[h % 2]
            kph = kp_h[h % 2]
            for dc in range(2):
                pg.op("sync", lambda e, h=h, dc=dc, qph=qph: e.dma_start(
                    qph[:, dc * 2048:(dc + 1) * 2048],
                    qp_fm_ap[h * 256 + dc * 128: h * 256 + (dc + 1) * 128, :]),
                    reads=[], writes=[W(qph, dc)], dma=True)
            for r in range(4):
                for dc in range(2):
                    for bi in range(4):
                        Bg = BLOCKS[r][bi]
                        pg.op("scalar", lambda e, h=h, r=r, dc=dc, bi=bi, Bg=Bg, kph=kph:
                              e.dma_start(
                                  kph[:, dc * 8192 + Bg * 512: dc * 8192 + (Bg + 1) * 512],
                                  kp_all_ap[r * 2048 + h * 256 + dc * 128:
                                            r * 2048 + h * 256 + (dc + 1) * 128,
                                            bi * 512:(bi + 1) * 512]),
                              reads=[], writes=[W(kph, (dc, Bg))], dma=True)
            for qt in range(16):
                for kc in range(16):
                    ps = ps_s[kc % 2]
                    for dc in range(2):
                        pg.op("tensor", lambda e, h=h, qt=qt, kc=kc, dc=dc, ps=ps,
                              qph=qph, kph=kph: e.matmul(
                                  ps[:],
                                  qph[:, dc * 2048 + qt * 128: dc * 2048 + (qt + 1) * 128],
                                  kph[:, dc * 8192 + kc * 512: dc * 8192 + (kc + 1) * 512],
                                  start=(dc == 0), stop=(dc == 1)),
                              reads=[R(qph, dc), R(kph, (dc, kc))],
                              writes=[W(ps)])
                    jk = junk[kc % 2]
                    pg.op("scalar", lambda e, kc=kc, ps=ps, jk=jk: e.activation(
                        jk[:], ps[:], AF.Relu, accum_out=sumcols[:, kc:kc + 1]),
                        reads=[R(ps)], writes=[W(jk), W(sumcols, kc)])
                pg.op("vector", lambda e: e.reduce_sum(
                    headsum[:], sumcols[:], axis=mybir.AxisListType.X),
                    reads=[R(sumcols)], writes=[W(headsum)])
                pg.op("vector", lambda e, h=h, qt=qt: e.scalar_tensor_tensor(
                    rel_acc[:, qt:qt + 1], headsum[:], w_bc[:, h:h + 1],
                    rel_acc[:, qt:qt + 1], OP.mult, OP.add),
                    reads=[R(headsum), R(w_bc), R(rel_acc, qt)],
                    writes=[W(rel_acc, qt)])
        pg.op("vector", lambda e: e.tensor_scalar(
            rel_acc[:], rel_acc[:], t_bc[:], None, OP.mult),
            reads=[R(rel_acc), R(t_bc)], writes=[W(rel_acc)])
        for lt in range(16):
            pg.op("sync", lambda e, lt=lt: e.dma_start(
                rel_ap[lt * 128:(lt + 1) * 128, :], rel_acc[:, lt:lt + 1]),
                reads=[R(rel_acc)], writes=[], dma=True)


RANK_OF_BLOCK = {}
POS_OF_BLOCK = {}
for _r in range(4):
    for _bi, _Bq in enumerate(BLOCKS[_r]):
        RANK_OF_BLOCK[_Bq] = _r
        POS_OF_BLOCK[_Bq] = _bi


def _ktcol(kt):
    Bq = kt // 4
    return RANK_OF_BLOCK[Bq] * 2048 + POS_OF_BLOCK[Bq] * 512 + (kt % 4) * 128


def _grp_L(bi):
    return 16 * bi + 16


def emit_bisect(pg: Prog, rel_sb, lo_bc, scr):
    identf = pg.ident_f32()
    ones = pg.ones_row()
    colred = scr["colred"]
    ps_t = scr["ps_t"]
    ps_c = scr["ps_c"]
    hmax = scr["hmax"]; hmin = scr["hmin"]; lo = scr["lo"]; hi = scr["hi"]
    mid = scr["mid"]; cnt = scr["cnt"]; pred = scr["pred"]; t1 = scr["t1"]
    ge = scr["ge"]
    ps_b1 = scr["ps_b1"]

    def vop(fn, reads, writes):
        pg.op("vector", fn, reads=reads, writes=writes)

    vop(lambda e: e.reduce_max(colred[:], rel_sb[:], axis=mybir.AxisListType.X),
        [R(rel_sb)], [W(colred)])
    pg.op("tensor", lambda e: e.transpose(ps_t[:], colred[:], identf[:]),
          reads=[R(colred), R(identf)], writes=[W(ps_t)])
    vop(lambda e: e.reduce_max(hmax[:], ps_t[:], axis=mybir.AxisListType.X),
        [R(ps_t)], [W(hmax)])
    vop(lambda e: e.tensor_reduce(colred[:], rel_sb[:], mybir.AxisListType.X, OP.min),
        [R(rel_sb)], [W(colred)])
    pg.op("tensor", lambda e: e.transpose(ps_t[:], colred[:], identf[:]),
          reads=[R(colred), R(identf)], writes=[W(ps_t)])
    vop(lambda e: e.tensor_reduce(hmin[:], ps_t[:], mybir.AxisListType.X, OP.min),
        [R(ps_t)], [W(hmin)])
    pg.op("scalar", lambda e: e.activation(t1[:], hmin[:], AF.Abs),
          reads=[R(hmin)], writes=[W(t1)])
    vop(lambda e: e.tensor_scalar(t1[:], t1[:], 1.0, 1e-3, OP.add, OP.mult),
        [R(t1)], [W(t1)])
    vop(lambda e: e.tensor_sub(lo[:], hmin[:], t1[:]), [R(hmin), R(t1)], [W(lo)])
    pg.op("scalar", lambda e: e.activation(t1[:], hmax[:], AF.Abs),
          reads=[R(hmax)], writes=[W(t1)])
    vop(lambda e: e.tensor_scalar(t1[:], t1[:], 1.0, 1e-3, OP.add, OP.mult),
        [R(t1)], [W(t1)])
    vop(lambda e: e.tensor_add(hi[:], hmax[:], t1[:]), [R(hmax), R(t1)], [W(hi)])

    for it in range(42):
        vop(lambda e: e.tensor_add(mid[:], lo[:], hi[:]), [R(lo), R(hi)], [W(mid)])
        vop(lambda e: e.tensor_scalar(mid[:], mid[:], 0.5, None, OP.mult),
            [R(mid)], [W(mid)])
        pg.op("tensor", lambda e: e.matmul(ps_b1[:], ones[:], mid[:], start=True, stop=True),
              reads=[R(ones), R(mid)], writes=[W(ps_b1)])
        vop(lambda e: e.tensor_copy(scr["mid_bc"][:], ps_b1[:]), [R(ps_b1)], [W(scr["mid_bc"])])
        vop(lambda e: e.tensor_scalar(ge[:], rel_sb[:], scr["mid_bc"][:], None, OP.is_ge),
            [R(rel_sb), R(scr["mid_bc"])], [W(ge)])
        vop(lambda e: e.reduce_sum(colred[:], ge[:], axis=mybir.AxisListType.X),
            [R(ge)], [W(colred)])
        pg.op("tensor", lambda e: e.matmul(ps_c[:], colred[:], scr["ones_col"][:],
                                           start=True, stop=True),
              reads=[R(colred), R(scr["ones_col"])], writes=[W(ps_c)])
        vop(lambda e: e.tensor_copy(cnt[:], ps_c[:]), [R(ps_c)], [W(cnt)])
        vop(lambda e: e.tensor_scalar(pred[:], cnt[:], 1024.0, None, OP.is_ge),
            [R(cnt)], [W(pred)])
        vop(lambda e: e.tensor_sub(t1[:], mid[:], lo[:]), [R(mid), R(lo)], [W(t1)])
        vop(lambda e: e.tensor_mul(t1[:], pred[:], t1[:]), [R(pred), R(t1)], [W(t1)])
        vop(lambda e: e.tensor_add(lo[:], lo[:], t1[:]), [R(lo), R(t1)], [W(lo)])
        vop(lambda e: e.tensor_sub(t1[:], hi[:], mid[:]), [R(hi), R(mid)], [W(t1)])
        vop(lambda e: e.tensor_mul(t1[:], pred[:], t1[:]), [R(pred), R(t1)], [W(t1)])
        vop(lambda e: e.tensor_add(hi[:], mid[:], t1[:]), [R(mid), R(t1)], [W(hi)])
    pg.op("tensor", lambda e: e.matmul(ps_b1[:], ones[:], lo[:], start=True, stop=True),
          reads=[R(ones), R(lo)], writes=[W(ps_b1)])
    vop(lambda e: e.tensor_copy(lo_bc[:], ps_b1[:]), [R(ps_b1)], [W(lo_bc)])


def emit_l0_attn(pg: Prog, q_fm_ap, k_all_ap, v_all_ap, rel_all_ap,
                 wo_ap, maskA_ap, maskC_ap, x_ap, xo_ap, attn_dram):
    s128 = 1.0 / math.sqrt(128.0)
    with pg.scope():
        recip_all = pg.sb("recip_all", [128, 256], F32)
        zero_col = pg.const_col(0.0)
        ones_bf = pg.sb("ones_bf", [128, 1], BF16)
        pg.op("vector", lambda e: e.memset(ones_bf[:], 1.0), reads=[], writes=[W(ones_bf)])

        with pg.scope():
            rel_sb = pg.sb("rel_sb", [128, 64], F32)
            sel_cols = pg.sb("sel_cols", [128, 64], F32)
            bias_cols = pg.sb("bias_cols", [128, 64], F32)
            lo_bc = pg.sb("lo_bc", [128, 1], F32)

            for r in range(4):
                for bi in range(4):
                    Bg = BLOCKS[r][bi]
                    pg.op("sync", lambda e, r=r, bi=bi, Bg=Bg: e.dma_start(
                        rel_sb[:, Bg * 4:(Bg + 1) * 4],
                        rel_all_ap[r * 2048 + bi * 512: r * 2048 + (bi + 1) * 512, 0:1]
                        .rearrange("(s p) o -> p (s o)", p=128),
                        allow_slow_non_contiguous=True),
                        reads=[], writes=[W(rel_sb, (r, bi))], dma=True)

            scr = {
                "colred": pg.sb("colred", [128, 1], F32),
                "ps_t": pg.ps("ps_t", [1, 128], F32),
                "ps_c": pg.ps("ps_c", [1, 1], F32),
                "ps_b1": pg.ps("ps_b1", [128, 1], F32),
                "hmax": pg.sb("hmax", [1, 1], F32), "hmin": pg.sb("hmin", [1, 1], F32),
                "lo": pg.sb("lo", [1, 1], F32), "hi": pg.sb("hi", [1, 1], F32),
                "mid": pg.sb("mid", [1, 1], F32), "cnt": pg.sb("cnt", [1, 1], F32),
                "pred": pg.sb("pred", [1, 1], F32), "t1": pg.sb("t1", [1, 1], F32),
                "ge": pg.sb("ge", [128, 64], F32),
                "mid_bc": pg.sb("mid_bc", [128, 1], F32),
                "ones_col": pg.sb("ones_col", [128, 1], F32),
            }
            pg.op("vector", lambda e: e.memset(scr["ones_col"][:], 1.0),
                  reads=[], writes=[W(scr["ones_col"])])
            emit_bisect(pg, rel_sb, lo_bc, scr)
            pg.op("vector", lambda e: e.tensor_scalar(
                sel_cols[:], rel_sb[:], lo_bc[:], None, OP.is_ge),
                reads=[R(rel_sb), R(lo_bc)], writes=[W(sel_cols)])
            pg.op("vector", lambda e: e.tensor_scalar(
                bias_cols[:], sel_cols[:], 1e9, -1e9, OP.mult, OP.add),
                reads=[R(sel_cols)], writes=[W(bias_cols)])

            mA = pg.sb("mA", [128, 20 * 512], BF16)
            mC = pg.sb("mC", [128, 20 * 512], BF16)
            tmpm = pg.sb("tmpm", [128, 512], BF16)
            allowed = [pg.sb(f"allowed{bi}", [128, 20 * 512], BF16) for bi in range(4)]
            for bi in range(4):
                L = _grp_L(bi)
                Wn = min(20, L)
                wstart = L - Wn
                pg.op("sync", lambda e, bi=bi, Wn=Wn: e.dma_start(
                    mA[:, :Wn * 512].rearrange("p (w c) -> p w c", w=Wn),
                    maskA_ap[bi, :Wn].rearrange("w p c -> p w c")),
                    reads=[], writes=[W(mA)], dma=True)
                pg.op("sync", lambda e, bi=bi, Wn=Wn: e.dma_start(
                    mC[:, :Wn * 512].rearrange("p (w c) -> p w c", w=Wn),
                    maskC_ap[bi, :Wn].rearrange("w p c -> p w c")),
                    reads=[], writes=[W(mC)], dma=True)
                for w in range(Wn):
                    kt = wstart + w
                    pg.op("vector", lambda e, w=w, kt=kt: e.tensor_scalar(
                        tmpm[:], mC[:, w * 512:(w + 1) * 512],
                        sel_cols[:, kt:kt + 1], None, OP.mult),
                        reads=[R(mC), R(sel_cols)], writes=[W(tmpm)])
                    pg.op("vector", lambda e, bi=bi, w=w: e.tensor_max(
                        allowed[bi][:, w * 512:(w + 1) * 512],
                        mA[:, w * 512:(w + 1) * 512], tmpm[:]),
                        reads=[R(mA), R(tmpm)], writes=[W(allowed[bi], w)])

            K_h = [pg.sb(f"K_h{i}", [128, 8192], BF16) for i in range(2)]
            V_h = [pg.sb(f"V_h{i}", [128, 8192], BF16) for i in range(2)]
            Q_h = [pg.sb(f"Q_h{i}", [128, 2048], BF16) for i in range(2)]
            P_T = [pg.sb(f"P_T{i}", [128, 512], BF16) for i in range(3)]
            So = [pg.sb(f"So{i}", [128, 512], BF16) for i in range(2)]
            ps_s = [pg.ps(f"ps_s{i}", [128, 512], F32) for i in range(2)]
            ps_o = pg.ps("ps_o", [128, 512], F32)
            ps_den = pg.ps("ps_den", [128, 4], F32)

            for h in range(16):
                kh, vh, qh = K_h[h % 2], V_h[h % 2], Q_h[h % 2]
                for r in range(4):
                    pg.op("sync", lambda e, h=h, r=r, kh=kh: e.dma_start(
                        kh[:, r * 2048:(r + 1) * 2048],
                        k_all_ap[r * 2048 + h * 128: r * 2048 + (h + 1) * 128, :]),
                        reads=[], writes=[W(kh, r)], dma=True)
                    pg.op("scalar", lambda e, h=h, r=r, vh=vh: e.dma_start(
                        vh[:, r * 2048:(r + 1) * 2048]
                        .rearrange("p (lt c) -> p lt c", lt=16),
                        v_all_ap[r * 2048:(r + 1) * 2048, h * 128:(h + 1) * 128]
                        .rearrange("(lt p) c -> p lt c", p=128)),
                        reads=[], writes=[W(vh, r)], dma=True)
                pg.op("sync", lambda e, h=h, qh=qh: e.dma_start(
                    qh[:], q_fm_ap[h * 128:(h + 1) * 128, :]),
                    reads=[], writes=[W(qh)], dma=True)
                for bi in range(4):
                    L = _grp_L(bi)
                    Wn = min(20, L)
                    wstart = L - Wn
                    for kt in range(L):
                        ps = ps_s[kt % 2]
                        pt = P_T[kt % 3]
                        kc = _ktcol(kt)
                        pg.op("tensor", lambda e, kc=kc, bi=bi, ps=ps, kh=kh, qh=qh:
                              e.matmul(ps[:], kh[:, kc:kc + 128],
                                       qh[:, bi * 512:(bi + 1) * 512],
                                       start=True, stop=True),
                              reads=[R(kh), R(qh)], writes=[W(ps)])
                        if kt < wstart:
                            pg.op("scalar", lambda e, ps=ps, pt=pt, kt=kt: e.activation(
                                pt[:], ps[:], AF.Exp, scale=s128,
                                bias=bias_cols[:, kt:kt + 1]),
                                reads=[R(ps), R(bias_cols)], writes=[W(pt)])
                        else:
                            pg.op("scalar", lambda e, ps=ps, pt=pt: e.activation(
                                pt[:], ps[:], AF.Exp, scale=s128, bias=zero_col[:]),
                                reads=[R(ps), R(zero_col)], writes=[W(pt)])
                            w = kt - wstart
                            pg.op("vector", lambda e, pt=pt, bi=bi, w=w: e.tensor_mul(
                                pt[:], pt[:], allowed[bi][:, w * 512:(w + 1) * 512]),
                                reads=[R(pt), R(allowed[bi], w)], writes=[W(pt)])
                        pg.op("tensor", lambda e, kc=kc, pt=pt, vh=vh, kt=kt, L=L:
                              e.matmul(ps_o[:], vh[:, kc:kc + 128], pt[:],
                                       start=(kt == 0), stop=(kt == L - 1)),
                              reads=[R(vh), R(pt)], writes=[W(ps_o)])
                        for si in range(4):
                            pg.op("tensor", lambda e, pt=pt, si=si, kt=kt, L=L:
                                  e.matmul(ps_den[:, si:si + 1],
                                           pt[:, si * 128:(si + 1) * 128], ones_bf[:],
                                           start=(kt == 0 and si == 0),
                                           stop=(kt == L - 1 and si == 3)),
                                  reads=[R(pt), R(ones_bf)], writes=[W(ps_den)])
                    cI = h * 16 + bi * 4
                    pg.op("vector", lambda e, cI=cI: e.reciprocal(
                        recip_all[:, cI:cI + 4], ps_den[:]),
                        reads=[R(ps_den)], writes=[W(recip_all, cI)])
                    so = So[bi % 2]
                    pg.op("scalar", lambda e, so=so: e.activation(so[:], ps_o[:], AF.Copy),
                          reads=[R(ps_o)], writes=[W(so)])
                    pg.op("sync", lambda e, h=h, bi=bi, so=so: e.dma_start(
                        attn_dram[h * 128:(h + 1) * 128, bi * 512:(bi + 1) * 512], so[:]),
                        reads=[R(so)], writes=[W(attn_dram, (h, bi))], dma=True)

        with pg.scope():
            ident = pg.ident_bf()
            attn_sb = pg.sb("attn_sb", [128, 16 * 2048], BF16)
            w_bf = pg.sb("w_bf2", [128, 16 * 2048], BF16)
            psT = [pg.ps(f"psTa{i}", [128, 128], BF16) for i in range(2)]
            psT2 = [pg.ps(f"psTb{i}", [128, 128], BF16) for i in range(2)]
            S2 = [pg.sb(f"S2_{i}", [128, 128], BF16) for i in range(2)]
            xt = [pg.sb(f"xta{i}", [128, 2048], F32) for i in range(2)]
            ps_w = [pg.ps(f"ps_w{i}", [128, 512], F32) for i in range(2)]

            for kc in range(16):
                pg.op("scalar", lambda e, kc=kc: e.dma_start(
                    attn_sb[:, kc * 2048:(kc + 1) * 2048],
                    attn_dram[kc * 128:(kc + 1) * 128, :]),
                    reads=[R(attn_dram)], writes=[W(attn_sb, ("h", kc))], dma=True)
            emit_wcast_load(pg, wo_ap, w_bf, 0, 0, 16, 2048)

            for h in range(16):
                for lq in range(16):
                    pt = psT[lq % 2]
                    pt2 = psT2[lq % 2]
                    s2 = S2[lq % 2]
                    col = h * 2048 + lq * 128
                    pg.op("tensor", lambda e, col=col, pt=pt: e.transpose(
                        pt[:], attn_sb[:, col:col + 128], ident[:]),
                        reads=[R(attn_sb), R(ident)],
                        writes=[W(pt)])
                    pg.op("vector", lambda e, pt=pt, s2=s2, h=h, lq=lq: e.tensor_scalar(
                        s2[:], pt[:], recip_all[:, h * 16 + lq: h * 16 + lq + 1],
                        None, OP.mult),
                        reads=[R(pt), R(recip_all, h * 16 + lq)], writes=[W(s2)])
                    pg.op("tensor", lambda e, s2=s2, pt2=pt2: e.transpose(
                        pt2[:], s2[:], ident[:]),
                        reads=[R(s2), R(ident)], writes=[W(pt2)])
                    pg.op("vector", lambda e, col=col, pt2=pt2: e.tensor_copy(
                        attn_sb[:, col:col + 128], pt2[:]),
                        reads=[R(pt2)], writes=[W(attn_sb)])

            for lt in range(16):
                xtt = xt[lt % 2]
                pg.op("sync", lambda e, lt=lt, xtt=xtt: e.dma_start(
                    xtt[:], x_ap[lt * 128:(lt + 1) * 128, :]),
                    reads=[], writes=[W(xtt)], dma=True)
                for oc in range(4):
                    ps = ps_w[oc % 2]
                    for h in range(16):
                        pg.op("tensor", lambda e, lt=lt, oc=oc, h=h, ps=ps: e.matmul(
                            ps[:], attn_sb[:, h * 2048 + lt * 128: h * 2048 + (lt + 1) * 128],
                            w_bf[:, h * 2048 + oc * 512: h * 2048 + (oc + 1) * 512],
                            start=(h == 0), stop=(h == 15)),
                            reads=[R(attn_sb), R(w_bf, h)], writes=[W(ps)])
                    pg.op("vector", lambda e, oc=oc, ps=ps, xtt=xtt: e.tensor_add(
                        xtt[:, oc * 512:(oc + 1) * 512],
                        ps[:], xtt[:, oc * 512:(oc + 1) * 512]),
                        reads=[R(ps), R(xtt, oc)], writes=[W(xtt, oc)])
                pg.op("sync", lambda e, lt=lt, xtt=xtt: e.dma_start(
                    xo_ap[lt * 128:(lt + 1) * 128, :], xtt[:]),
                    reads=[R(xtt)], writes=[], dma=True)


def build_p4_l0attn():
    nc = bass.Bass("TRN2", target_bir_lowering=False)
    q_fm = nc.dram_tensor("q_fm", [D, TOK], BF16, kind="ExternalInput").ap()
    k_all = nc.dram_tensor("k_all", [4 * D, TOK], BF16, kind="ExternalInput").ap()
    v_all = nc.dram_tensor("v_all", [4 * TOK, D], BF16, kind="ExternalInput").ap()
    rel_all = nc.dram_tensor("rel_all", [4 * TOK, 1], F32, kind="ExternalInput").ap()
    wo = nc.dram_tensor("wo", [D, D], F32, kind="ExternalInput").ap()
    maskA = nc.dram_tensor("maskA", [4, 20, 128, 512], BF16, kind="ExternalInput").ap()
    maskC = nc.dram_tensor("maskC", [4, 20, 128, 512], BF16, kind="ExternalInput").ap()
    x = nc.dram_tensor("x", [TOK, D], F32, kind="ExternalInput").ap()
    xo = nc.dram_tensor("xo", [TOK, D], F32, kind="ExternalOutput").ap()
    attn_dram = nc.dram_tensor("attn_scratch", [D, TOK], BF16, kind="Internal").ap()
    pg = Prog(nc)
    emit_l0_attn(pg, q_fm, k_all, v_all, rel_all, wo, maskA, maskC, x, xo, attn_dram)
    pg.barrier()
    pg.finalize()
    return nc


def host_masks(g):
    maskA = np.zeros((4, 20, 128, 512), dtype=ml_dtypes.bfloat16)
    maskC = np.zeros((4, 20, 128, 512), dtype=ml_dtypes.bfloat16)
    p = np.arange(128)[:, None]
    f = np.arange(512)[None, :]
    for bi in range(4):
        L = _grp_L(bi)
        Wn = min(20, L)
        wstart = L - Wn
        Bo = BLOCKS[g][bi]
        for w in range(Wn):
            kt = wstart + w
            Dd = (kt * 128 + p) - (Bo * 512 + f)
            C = (Dd <= 0)
            A = C & (Dd >= -512)
            maskA[bi, w] = A.astype(ml_dtypes.bfloat16)
            maskC[bi, w] = C.astype(ml_dtypes.bfloat16)
    return maskA, maskC


def emit_moe(pg: Prog, x_ap, gate_r_ap, wi0_ap, wi1_ap, wo_ap, xo_ap, dbg=None):
    with pg.scope():
        ident = pg.ident_bf()
        identf = pg.ident_f32()
        eps_col = pg.const_col(EPS)
        g_sb = pg.sb("g_sb", [128, 128], F32)
        pg.op("sync", lambda e: e.dma_start(g_sb[:], gate_r_ap),
              reads=[], writes=[W(g_sb)], dma=True)

        xnF_m = pg.sb("xnF_m", [128, 16 * 1024], BF16)
        s0 = pg.sb("s0_moe", [128, 16 * 1024], BF16)
        out_acc = pg.sb("out_acc", [128, 8 * 2048], BF16)
        wbuf = [pg.sb(f"wbuf{j}", [128, 16 * 1024], BF16) for j in range(2)]
        cw_all = pg.sb("cw_all", [128, 64], F32)
        ps_h = [pg.ps(f"ps_h{i}", [128, 512], F32) for i in range(2)]
        ps_w = [pg.ps(f"ps_wm{i}", [128, 512], F32) for i in range(2)]

        for mblk in range(2):
            with pg.scope():
                xt = [pg.sb(f"xtm{i}", [128, 2048], F32) for i in range(2)]
                xn32 = pg.sb("xn32", [128, 2048], F32)
                xnb = [pg.sb(f"xnbm{i}", [128, 2048], BF16) for i in range(2)]
                stats = pg.sb("statsm", [128, 24], F32)
                aggr = [pg.sb(f"aggrm{i}", [128, 2], F32) for i in range(2)]
                rstd = [pg.sb(f"rstdm{i}", [128, 1], F32) for i in range(2)]
                xf32t = [pg.sb(f"xf32t{i}", [128, 128], F32) for i in range(2)]
                psT_g = pg.ps("psT_g", [128, 128], F32)
                ps_g = pg.ps("ps_g", [128, 8], F32)
                psT = [pg.ps(f"psTm{i}", [128, 128], BF16) for i in range(2)]
                gl = pg.sb("gl", [128, 8], F32)
                m1 = pg.sb("m1", [128, 1], F32)
                m2 = pg.sb("m2", [128, 1], F32)
                ind1 = pg.sb("ind1", [128, 8], F32)
                ind2 = pg.sb("ind2", [128, 8], F32)
                msk = pg.sb("msk", [128, 8], F32)
                w1h = pg.sb("w1h", [128, 1], F32)
                w2h = pg.sb("w2h", [128, 1], F32)
                tg = pg.sb("tg", [128, 8], F32)

                for i in range(8):
                    lt = mblk * 8 + i
                    xtt = xt[i % 2]
                    xnn = xnb[i % 2]
                    ag = aggr[i % 2]
                    rs = rstd[i % 2]
                    pg.op("sync", lambda e, lt=lt, xtt=xtt: e.dma_start(
                        xtt[:], x_ap[lt * 128:(lt + 1) * 128, :]),
                        reads=[], writes=[W(xtt)], dma=True)
                    emit_ln_tile(pg, xtt, xnn, stats, ag, rs, eps_col)
                    pg.op("vector", lambda e, xtt=xtt, ag=ag, rs=rs: e.tensor_scalar(
                        xn32[:], xtt[:], ag[:, 0:1], rs[:], OP.subtract, OP.mult),
                        reads=[R(xtt), R(ag), R(rs)], writes=[W(xn32)])
                    for dc in range(16):
                        xf = xf32t[dc % 2]
                        pg.op("tensor", lambda e, dc=dc: e.transpose(
                            psT_g[:], xn32[:, dc * 128:(dc + 1) * 128], identf[:]),
                            reads=[R(xn32), R(identf)], writes=[W(psT_g)])
                        pg.op("vector", lambda e, xf=xf: e.tensor_copy(xf[:], psT_g[:]),
                              reads=[R(psT_g)], writes=[W(xf)])
                        pg.op("tensor", lambda e, dc=dc, xf=xf: e.matmul(
                            ps_g[:], xf[:], g_sb[:, dc * 8:(dc + 1) * 8],
                            start=(dc == 0), stop=(dc == 15)),
                            reads=[R(xf), R(g_sb)], writes=[W(ps_g)])
                    pg.op("vector", lambda e: e.tensor_copy(gl[:], ps_g[:]),
                          reads=[R(ps_g)], writes=[W(gl)])
                    pg.op("vector", lambda e: e.reduce_max(m1[:], gl[:], axis=mybir.AxisListType.X),
                          reads=[R(gl)], writes=[W(m1)])
                    pg.op("vector", lambda e: e.tensor_scalar(
                        ind1[:], gl[:], m1[:], None, OP.is_ge),
                        reads=[R(gl), R(m1)], writes=[W(ind1)])
                    pg.op("vector", lambda e: e.tensor_scalar(
                        tg[:], ind1[:], 1e30, None, OP.mult),
                        reads=[R(ind1)], writes=[W(tg)])
                    pg.op("vector", lambda e: e.tensor_sub(msk[:], gl[:], tg[:]),
                          reads=[R(gl), R(tg)], writes=[W(msk)])
                    pg.op("vector", lambda e: e.reduce_max(m2[:], msk[:], axis=mybir.AxisListType.X),
                          reads=[R(msk)], writes=[W(m2)])
                    pg.op("vector", lambda e: e.tensor_scalar(
                        ind2[:], msk[:], m2[:], None, OP.is_ge),
                        reads=[R(msk), R(m2)], writes=[W(ind2)])
                    pg.op("vector", lambda e: e.tensor_sub(w1h[:], m2[:], m1[:]),
                          reads=[R(m2), R(m1)], writes=[W(w1h)])
                    pg.op("scalar", lambda e: e.activation(w1h[:], w1h[:], AF.Exp),
                          reads=[R(w1h)], writes=[W(w1h)])
                    pg.op("vector", lambda e: e.tensor_scalar(
                        w1h[:], w1h[:], 1.0, None, OP.add),
                        reads=[R(w1h)], writes=[W(w1h)])
                    pg.op("vector", lambda e: e.reciprocal(w1h[:], w1h[:]),
                          reads=[R(w1h)], writes=[W(w1h)])
                    pg.op("vector", lambda e: e.tensor_scalar(
                        w1h[:], w1h[:], 0.5, None, OP.mult),
                        reads=[R(w1h)], writes=[W(w1h)])
                    pg.op("vector", lambda e: e.tensor_scalar(
                        w2h[:], w1h[:], -1.0, 0.5, OP.mult, OP.add),
                        reads=[R(w1h)], writes=[W(w2h)])
                    pg.op("vector", lambda e, i=i: e.tensor_scalar(
                        cw_all[:, i * 8:(i + 1) * 8], ind1[:], w1h[:], None, OP.mult),
                        reads=[R(ind1), R(w1h)], writes=[W(cw_all, (i, 1))])
                    pg.op("vector", lambda e, i=i: e.tensor_scalar(
                        tg[:], ind2[:], w2h[:], None, OP.mult),
                        reads=[R(ind2), R(w2h)], writes=[W(tg)])
                    pg.op("vector", lambda e, i=i: e.tensor_add(
                        cw_all[:, i * 8:(i + 1) * 8],
                        cw_all[:, i * 8:(i + 1) * 8], tg[:]),
                        reads=[R(cw_all, (i, 1)), R(tg)], writes=[W(cw_all, (i, 2))])
                    for kc in range(16):
                        emit_transpose_to(pg, xnn, kc * 128, xnF_m,
                                          kc * 1024 + i * 128, psT[kc % 2], ident)

            for e_i in range(8):
                loads = [(wi0_ap, 0), (wi0_ap, 1), (wi1_ap, 0), (wi1_ap, 1),
                         (wo_ap, 0), (wo_ap, 1)]
                bufs = [wbuf[i % 2] for i in range(6)]

                def emit_load(li, e_i=e_i, loads=loads, bufs=bufs):
                    wap, half = loads[li]
                    buf = bufs[li]
                    for kc in range(16):
                        pg.op("gpsimd", lambda e, wap=wap, half=half, kc=kc, buf=buf,
                              e_i=e_i: e.dma_start(
                                  buf[:, kc * 1024:(kc + 1) * 1024],
                                  wap[e_i, kc * 128:(kc + 1) * 128,
                                      half * 1024:(half + 1) * 1024]),
                              reads=[], writes=[W(buf, kc)], dma=True)

                emit_load(0)
                emit_load(1)

                def wslice(li, kc, lcol, n):
                    return bufs[li][:, kc * 1024 + lcol: kc * 1024 + lcol + n]

                for mc in range(16):
                    li = mc // 8
                    lcol = (mc % 8) * 128
                    for tc in range(2):
                        ps = ps_h[tc]
                        for k in range(16):
                            pg.op("tensor", lambda e, li=li, k=k, lcol=lcol, tc=tc, ps=ps,
                                  ws=wslice:
                                  e.matmul(ps[:], ws(li, k, lcol, 128),
                                           xnF_m[:, k * 1024 + tc * 512: k * 1024 + (tc + 1) * 512],
                                           start=(k == 0), stop=(k == 15)),
                                  reads=[R(bufs[li], k), R(xnF_m)], writes=[W(ps)])
                        pg.op("scalar", lambda e, mc=mc, tc=tc, ps=ps: e.activation(
                            s0[:, mc * 1024 + tc * 512: mc * 1024 + (tc + 1) * 512],
                            ps[:], AF.Silu),
                            reads=[R(ps)], writes=[W(s0, (mc, tc))])
                if dbg is not None and mblk == 0 and e_i == 0:
                    pg.op("sync", lambda e: e.dma_start(dbg["s0h0"], s0[:]),
                          reads=[R(s0)], writes=[], dma=True)
                    pg.op("sync", lambda e: e.dma_start(dbg["xnF"], xnF_m[:]),
                          reads=[R(xnF_m)], writes=[], dma=True)
                    pg.op("sync", lambda e: e.dma_start(dbg["cw"], cw_all[:]),
                          reads=[R(cw_all)], writes=[], dma=True)
                emit_load(2)
                emit_load(3)
                for mc in range(16):
                    li = 2 + mc // 8
                    lcol = (mc % 8) * 128
                    for tc in range(2):
                        ps = ps_h[tc]
                        for k in range(16):
                            pg.op("tensor", lambda e, li=li, k=k, lcol=lcol, tc=tc, ps=ps,
                                  ws=wslice:
                                  e.matmul(ps[:], ws(li, k, lcol, 128),
                                           xnF_m[:, k * 1024 + tc * 512: k * 1024 + (tc + 1) * 512],
                                           start=(k == 0), stop=(k == 15)),
                                  reads=[R(bufs[li], k), R(xnF_m)], writes=[W(ps)])
                        sl = s0[:, mc * 1024 + tc * 512: mc * 1024 + (tc + 1) * 512]
                        pg.op("vector", lambda e, sl=sl, ps=ps: e.tensor_mul(
                            sl, ps[:], sl),
                            reads=[R(ps), R(s0, (mc, tc))], writes=[W(s0, (mc, tc))])
                if dbg is not None and mblk == 0 and e_i == 0:
                    pg.op("sync", lambda e: e.dma_start(dbg["s0h1"], s0[:]),
                          reads=[R(s0)], writes=[], dma=True)
                emit_load(4)
                emit_load(5)
                for i in range(8):
                    for oc in range(4):
                        li = 4 + oc // 2
                        ocl = (oc % 2) * 512
                        ps = ps_w[oc % 2]
                        for mc in range(16):
                            pg.op("tensor", lambda e, li=li, mc=mc, i=i, ocl=ocl, ps=ps,
                                  ws=wslice:
                                  e.matmul(ps[:], s0[:, mc * 1024 + i * 128: mc * 1024 + i * 128 + 128],
                                           ws(li, mc, ocl, 512),
                                           start=(mc == 0), stop=(mc == 15)),
                                  reads=[R(s0), R(bufs[li], mc)], writes=[W(ps)])
                        dst = out_acc[:, i * 2048 + oc * 512: i * 2048 + (oc + 1) * 512]
                        cwc = cw_all[:, i * 8 + e_i: i * 8 + e_i + 1]
                        if e_i == 0:
                            pg.op("vector", lambda e, dst=dst, cwc=cwc, ps=ps:
                                  e.tensor_scalar(dst, ps[:], cwc, None, OP.mult),
                                  reads=[R(ps), R(cw_all)], writes=[W(out_acc, (i, oc))])
                        else:
                            pg.op("vector", lambda e, dst=dst, cwc=cwc, ps=ps:
                                  e.scalar_tensor_tensor(dst, ps[:], cwc, dst, OP.mult, OP.add),
                                  reads=[R(ps), R(cw_all), R(out_acc, (i, oc))],
                                  writes=[W(out_acc, (i, oc))])
            with pg.scope():
                xt2 = [pg.sb(f"xtr{i}", [128, 2048], F32) for i in range(2)]
                for i in range(8):
                    lt = mblk * 8 + i
                    xtt = xt2[i % 2]
                    pg.op("sync", lambda e, lt=lt, xtt=xtt: e.dma_start(
                        xtt[:], x_ap[lt * 128:(lt + 1) * 128, :]),
                        reads=[], writes=[W(xtt)], dma=True)
                    pg.op("vector", lambda e, i=i, xtt=xtt: e.tensor_add(
                        xtt[:], xtt[:], out_acc[:, i * 2048:(i + 1) * 2048]),
                        reads=[R(xtt), R(out_acc, (i, 0)), R(out_acc, (i, 1)),
                               R(out_acc, (i, 2)), R(out_acc, (i, 3))],
                        writes=[W(xtt)])
                    pg.op("sync", lambda e, lt=lt, xtt=xtt: e.dma_start(
                        xo_ap[lt * 128:(lt + 1) * 128, :], xtt[:]),
                        reads=[R(xtt)], writes=[], dma=True)


def build_p6_moe(debug=False):
    nc = bass.Bass("TRN2", target_bir_lowering=False)
    x = nc.dram_tensor("x", [TOK, D], F32, kind="ExternalInput").ap()
    gater = nc.dram_tensor("gater", [128, 128], F32, kind="ExternalInput").ap()
    wi0 = nc.dram_tensor("wi0", [8, D, D], F32, kind="ExternalInput").ap()
    wi1 = nc.dram_tensor("wi1", [8, D, D], F32, kind="ExternalInput").ap()
    wo = nc.dram_tensor("wo", [8, D, D], F32, kind="ExternalInput").ap()
    xo = nc.dram_tensor("xo", [TOK, D], F32, kind="ExternalOutput").ap()
    dbg = None
    if debug:
        dbg = {
            "s0h0": nc.dram_tensor("dbg_s0h0", [128, 16 * 1024], BF16, kind="ExternalOutput").ap(),
            "s0h1": nc.dram_tensor("dbg_s0h1", [128, 16 * 1024], BF16, kind="ExternalOutput").ap(),
            "xnF": nc.dram_tensor("dbg_xnF", [128, 16 * 1024], BF16, kind="ExternalOutput").ap(),
            "cw": nc.dram_tensor("dbg_cw", [128, 64], F32, kind="ExternalOutput").ap(),
        }
    pg = Prog(nc)
    emit_moe(pg, x, gater, wi0, wi1, wo, xo, dbg=dbg)
    pg.barrier()
    pg.finalize()
    return nc


def emit_l1_proj(pg: Prog, x_ap, wqkvT_ap, bcols_ap, bvrow_ap,
                 q_fm_ap, k_fm_ap, v_tm_ap):
    with pg.scope():
        ident = pg.ident_bf()
        ones = pg.ones_row()
        eps_col = pg.const_col(EPS)
        xnF = pg.sb("xnF_l1", [128, 16 * 2048], BF16)
        with pg.scope():
            xt = [pg.sb(f"xtp{i}", [128, 2048], F32) for i in range(2)]
            xn = [pg.sb(f"xnp{i}", [128, 2048], BF16) for i in range(2)]
            stats = pg.sb("statsp", [128, 24], F32)
            aggr = [pg.sb(f"aggrp{i}", [128, 2], F32) for i in range(2)]
            rstd = [pg.sb(f"rstdp{i}", [128, 1], F32) for i in range(2)]
            psT = [pg.ps(f"psTp{i}", [128, 128], BF16) for i in range(2)]
            emit_ln_all(pg, x_ap, xnF, xt, xn, stats, aggr, rstd, eps_col, psT, ident)

        bcols = pg.sb("bcols_sb", [128, 48], F32)
        pg.op("sync", lambda e: e.dma_start(bcols[:], bcols_ap),
              reads=[], writes=[W(bcols)], dma=True)
        bvrow = pg.sb("bvrow_sb", [1, 2048], F32)
        pg.op("sync", lambda e: e.dma_start(bvrow[:], bvrow_ap),
              reads=[], writes=[W(bvrow)], dma=True)
        bv_bc = pg.sb("bv_bc", [128, 2048], F32)
        ps_bb = pg.ps("ps_bbv", [128, 512], F32)
        for oc in range(4):
            pg.op("tensor", lambda e, oc=oc: e.matmul(
                ps_bb[:], ones[:], bvrow[:, oc * 512:(oc + 1) * 512],
                start=True, stop=True),
                reads=[R(ones), R(bvrow)], writes=[W(ps_bb)])
            pg.op("vector", lambda e, oc=oc: e.tensor_copy(
                bv_bc[:, oc * 512:(oc + 1) * 512], ps_bb[:]),
                reads=[R(ps_bb)], writes=[W(bv_bc, oc)])

        w_bf = pg.sb("w_bf_l1", [128, 16 * 2048], BF16)
        ps_out = [pg.ps(f"psPq{i}", [128, 512], F32) for i in range(4)]
        stage = [pg.sb(f"stgq{i}", [128, 512], BF16) for i in range(4)]

        def store_fm_bias(dst_ap, sec):
            def cb(oc, tc, ps, stg):
                pg.op("vector", lambda e, oc=oc, ps=ps, stg=stg: e.tensor_scalar(
                    stg[:], ps[:], bcols[:, sec * 16 + oc: sec * 16 + oc + 1],
                    None, OP.add),
                    reads=[R(ps), R(bcols)], writes=[W(stg)])
                pg.op("sync", lambda e, oc=oc, tc=tc, stg=stg: e.dma_start(
                    dst_ap[oc * 128:(oc + 1) * 128, tc * 512:(tc + 1) * 512], stg[:]),
                    reads=[R(stg)], writes=[], dma=True)
            return cb

        emit_wcast_load(pg, wqkvT_ap, w_bf, 0, 0, 16, 2048)
        emit_proj_fm(pg, xnF, w_bf, store_fm_bias(q_fm_ap, 0), ps_out, stage)
        emit_wcast_load(pg, wqkvT_ap, w_bf, 0, 2048, 16, 2048)
        emit_proj_fm(pg, xnF, w_bf, store_fm_bias(k_fm_ap, 1), ps_out, stage)
        emit_wcast_load(pg, wqkvT_ap, w_bf, 0, 4096, 16, 2048)
        vstage = pg.sb("vstage_l1", [128, 2048], BF16)
        for lt in range(16):
            for oc in range(4):
                ps = ps_out[oc]
                for k in range(16):
                    pg.op("tensor", lambda e, lt=lt, oc=oc, k=k, ps=ps: e.matmul(
                        ps[:],
                        xnF[:, k * 2048 + lt * 128: k * 2048 + (lt + 1) * 128],
                        w_bf[:, k * 2048 + oc * 512: k * 2048 + (oc + 1) * 512],
                        start=(k == 0), stop=(k == 15)),
                        reads=[R(w_bf, k), R(xnF)], writes=[W(ps)])
                pg.op("vector", lambda e, oc=oc, ps=ps: e.scalar_tensor_tensor(
                    vstage[:, oc * 512:(oc + 1) * 512], ps[:], 1.0,
                    bv_bc[:, oc * 512:(oc + 1) * 512], OP.mult, OP.add),
                    reads=[R(ps), R(bv_bc, oc)], writes=[W(vstage, oc)])
            pg.op("sync", lambda e, lt=lt: e.dma_start(
                v_tm_ap[lt * 128:(lt + 1) * 128, :], vstage[:]),
                reads=[R(vstage)], writes=[], dma=True)


def emit_l1_attn(pg: Prog, q_fm_ap, k_all_ap, v_all_ap, woT_ap, bout_ap,
                 x_ap, xo_ap):
    s2048 = 1.0 / math.sqrt(2048.0)
    with pg.scope():
        ones = pg.ones_row()
        zero_col = pg.const_col(0.0)
        ones_bf = pg.sb("ones_bf_l1", [128, 1], BF16)
        pg.op("vector", lambda e: e.memset(ones_bf[:], 1.0), reads=[], writes=[W(ones_bf)])
        attn_sb = pg.sb("attn_l1", [128, 16 * 2048], BF16)
        recip_all = pg.sb("recip_l1", [128, 16], F32)

        with pg.scope():
            P_T = pg.sb("P_T_l1", [128, 64 * 512], BF16)
            Q_g = pg.sb("Q_g", [128, 16 * 512], BF16)
            K_kt = [pg.sb(f"K_kt{i}", [128, 16 * 128], BF16) for i in range(2)]
            V_dc = [pg.sb(f"V_dc{i}", [128, 64 * 128], BF16) for i in range(2)]
            ps_s = [pg.ps(f"ps_sl{i}", [128, 512], F32) for i in range(2)]
            ps_o = [pg.ps(f"ps_ol{i}", [128, 512], F32) for i in range(2)]
            ps_den = pg.ps("ps_den_l1", [128, 4], F32)

            for bi in range(4):
                pg.op("sync", lambda e, bi=bi: e.dma_start(
                    Q_g[:].rearrange("p (dc c) -> p dc c", dc=16),
                    q_fm_ap[:, bi * 512:(bi + 1) * 512]
                    .rearrange("(dc p) c -> p dc c", p=128)),
                    reads=[], writes=[W(Q_g)], dma=True)
                for kt in range(64):
                    kb = K_kt[kt % 2]
                    rk = RANK_OF_BLOCK[kt // 4]
                    lcol = POS_OF_BLOCK[kt // 4] * 512 + (kt % 4) * 128
                    pg.op("sync", lambda e, rk=rk, lcol=lcol, kb=kb: e.dma_start(
                        kb[:].rearrange("p (dc c) -> p dc c", dc=16),
                        k_all_ap[rk * 2048: (rk + 1) * 2048, lcol:lcol + 128]
                        .rearrange("(dc p) c -> p dc c", p=128)),
                        reads=[], writes=[W(kb)], dma=True)
                    ps = ps_s[kt % 2]
                    for dc in range(16):
                        pg.op("tensor", lambda e, dc=dc, kb=kb, ps=ps: e.matmul(
                            ps[:], kb[:, dc * 128:(dc + 1) * 128],
                            Q_g[:, dc * 512:(dc + 1) * 512],
                            start=(dc == 0), stop=(dc == 15)),
                            reads=[R(kb), R(Q_g)], writes=[W(ps)])
                    pg.op("scalar", lambda e, kt=kt, ps=ps: e.activation(
                        P_T[:, kt * 512:(kt + 1) * 512], ps[:], AF.Exp,
                        scale=s2048, bias=zero_col[:]),
                        reads=[R(ps), R(zero_col)], writes=[W(P_T, kt)])
                    for si in range(4):
                        pg.op("tensor", lambda e, kt=kt, si=si: e.matmul(
                            ps_den[:, si:si + 1],
                            P_T[:, kt * 512 + si * 128: kt * 512 + (si + 1) * 128],
                            ones_bf[:],
                            start=(kt == 0 and si == 0),
                            stop=(kt == 63 and si == 3)),
                            reads=[R(P_T, kt), R(ones_bf)], writes=[W(ps_den)])
                pg.op("vector", lambda e, bi=bi: e.reciprocal(
                    recip_all[:, bi * 4:(bi + 1) * 4], ps_den[:]),
                    reads=[R(ps_den)], writes=[W(recip_all, bi)])
                for dc in range(16):
                    vb = V_dc[dc % 2]
                    for r in range(4):
                        pg.op("scalar", lambda e, r=r, dc=dc, vb=vb: e.dma_start(
                            vb[:, r * 2048:(r + 1) * 2048]
                            .rearrange("p (lt c) -> p lt c", lt=16),
                            v_all_ap[r * 2048:(r + 1) * 2048, dc * 128:(dc + 1) * 128]
                            .rearrange("(lt p) c -> p lt c", p=128)),
                            reads=[], writes=[W(vb, r)], dma=True)
                    ps = ps_o[dc % 2]
                    for kt in range(64):
                        vt = RANK_OF_BLOCK[kt // 4] * 16 + POS_OF_BLOCK[kt // 4] * 4 + kt % 4
                        pg.op("tensor", lambda e, vt=vt, kt=kt, vb=vb, ps=ps: e.matmul(
                            ps[:], vb[:, vt * 128:(vt + 1) * 128],
                            P_T[:, kt * 512:(kt + 1) * 512],
                            start=(kt == 0), stop=(kt == 63)),
                            reads=[R(vb), R(P_T, kt)], writes=[W(ps)])
                    pg.op("scalar", lambda e, dc=dc, bi=bi, ps=ps: e.activation(
                        attn_sb[:, dc * 2048 + bi * 512: dc * 2048 + (bi + 1) * 512],
                        ps[:], AF.Copy),
                        reads=[R(ps)], writes=[W(attn_sb, (dc, bi))])

        with pg.scope():
            w_bf = pg.sb("w_bf_wo", [128, 16 * 2048], BF16)
            bout_row = pg.sb("bout_row_sb", [1, 2048], F32)
            bout_bc = pg.sb("bout_bc", [128, 2048], F32)
            ps_bb = pg.ps("ps_bbo", [128, 512], F32)
            xt = [pg.sb(f"xtl{i}", [128, 2048], F32) for i in range(2)]
            tstage = [pg.sb(f"tstg{i}", [128, 512], F32) for i in range(2)]
            ps_w = [pg.ps(f"ps_wl{i}", [128, 512], F32) for i in range(2)]

            pg.op("sync", lambda e: e.dma_start(bout_row[:], bout_ap),
                  reads=[], writes=[W(bout_row)], dma=True)
            for oc in range(4):
                pg.op("tensor", lambda e, oc=oc: e.matmul(
                    ps_bb[:], ones[:], bout_row[:, oc * 512:(oc + 1) * 512],
                    start=True, stop=True),
                    reads=[R(ones), R(bout_row)], writes=[W(ps_bb)])
                pg.op("vector", lambda e, oc=oc: e.tensor_copy(
                    bout_bc[:, oc * 512:(oc + 1) * 512], ps_bb[:]),
                    reads=[R(ps_bb)], writes=[W(bout_bc, oc)])
            emit_wcast_load(pg, woT_ap, w_bf, 0, 0, 16, 2048)

            for lt in range(16):
                xtt = xt[lt % 2]
                pg.op("sync", lambda e, lt=lt, xtt=xtt: e.dma_start(
                    xtt[:], x_ap[lt * 128:(lt + 1) * 128, :]),
                    reads=[], writes=[W(xtt)], dma=True)
                for oc in range(4):
                    ps = ps_w[oc % 2]
                    for dc in range(16):
                        pg.op("tensor", lambda e, lt=lt, oc=oc, dc=dc, ps=ps: e.matmul(
                            ps[:],
                            attn_sb[:, dc * 2048 + lt * 128: dc * 2048 + (lt + 1) * 128],
                            w_bf[:, dc * 2048 + oc * 512: dc * 2048 + (oc + 1) * 512],
                            start=(dc == 0), stop=(dc == 15)),
                            reads=[R(attn_sb), R(w_bf, dc)], writes=[W(ps)])
                    ts = tstage[oc % 2]
                    pg.op("vector", lambda e, oc=oc, lt=lt, ps=ps, ts=ts:
                          e.scalar_tensor_tensor(
                              ts[:], ps[:], recip_all[:, lt:lt + 1],
                              bout_bc[:, oc * 512:(oc + 1) * 512], OP.mult, OP.add),
                          reads=[R(ps), R(recip_all), R(bout_bc, oc)], writes=[W(ts)])
                    pg.op("vector", lambda e, oc=oc, ts=ts, xtt=xtt: e.tensor_add(
                        xtt[:, oc * 512:(oc + 1) * 512],
                        xtt[:, oc * 512:(oc + 1) * 512], ts[:]),
                        reads=[R(ts), R(xtt, oc)], writes=[W(xtt, oc)])
                pg.op("sync", lambda e, lt=lt, xtt=xtt: e.dma_start(
                    xo_ap[lt * 128:(lt + 1) * 128, :], xtt[:]),
                    reads=[R(xtt)], writes=[], dma=True)


def build_p7_l1proj():
    nc = bass.Bass("TRN2", target_bir_lowering=False)
    x = nc.dram_tensor("x", [TOK, D], F32, kind="ExternalInput").ap()
    wqkvT = nc.dram_tensor("wqkvT", [D, 3 * D], F32, kind="ExternalInput").ap()
    bcols = nc.dram_tensor("bcols", [128, 48], F32, kind="ExternalInput").ap()
    bvrow = nc.dram_tensor("bvrow", [1, D], F32, kind="ExternalInput").ap()
    q_fm = nc.dram_tensor("q_fm", [D, TOK], BF16, kind="ExternalOutput").ap()
    k_fm = nc.dram_tensor("k_fm", [D, TOK], BF16, kind="ExternalOutput").ap()
    v_tm = nc.dram_tensor("v_tm", [TOK, D], BF16, kind="ExternalOutput").ap()
    pg = Prog(nc)
    emit_l1_proj(pg, x, wqkvT, bcols, bvrow, q_fm, k_fm, v_tm)
    pg.barrier()
    pg.finalize()
    return nc


def build_p8_l1attn():
    nc = bass.Bass("TRN2", target_bir_lowering=False)
    q_fm = nc.dram_tensor("q_fm", [D, TOK], BF16, kind="ExternalInput").ap()
    k_all = nc.dram_tensor("k_all", [4 * D, TOK], BF16, kind="ExternalInput").ap()
    v_all = nc.dram_tensor("v_all", [4 * TOK, D], BF16, kind="ExternalInput").ap()
    woT = nc.dram_tensor("woT", [D, D], F32, kind="ExternalInput").ap()
    bout = nc.dram_tensor("bout", [1, D], F32, kind="ExternalInput").ap()
    x = nc.dram_tensor("x", [TOK, D], F32, kind="ExternalInput").ap()
    xo = nc.dram_tensor("xo", [TOK, D], F32, kind="ExternalOutput").ap()
    pg = Prog(nc)
    emit_l1_attn(pg, q_fm, k_all, v_all, woT, bout, x, xo)
    pg.barrier()
    pg.finalize()
    return nc


def build_p1_ffn0():
    nc = bass.Bass("TRN2", target_bir_lowering=False)
    x = nc.dram_tensor("x", [TOK, D], F32, kind="ExternalInput").ap()
    factors = nc.dram_tensor("factors", [8, D, D], F32, kind="ExternalInput").ap()
    fbias = nc.dram_tensor("fbias", [1, D], F32, kind="ExternalInput").ap()
    xo = nc.dram_tensor("xo", [TOK, D], F32, kind="ExternalOutput").ap()
    pg = Prog(nc)
    emit_ffn0(pg, x, factors, fbias, xo)
    pg.barrier()
    pg.finalize()
    return nc


def build_p2_l0proj():
    nc = bass.Bass("TRN2", target_bir_lowering=False)
    x = nc.dram_tensor("x", [TOK, D], F32, kind="ExternalInput").ap()
    wq = nc.dram_tensor("wq", [D, D], F32, kind="ExternalInput").ap()
    wk = nc.dram_tensor("wk", [D, D], F32, kind="ExternalInput").ap()
    wv = nc.dram_tensor("wv", [D, D], F32, kind="ExternalInput").ap()
    wqi = nc.dram_tensor("wqi", [D, D], F32, kind="ExternalInput").ap()
    wki = nc.dram_tensor("wki", [D, D], F32, kind="ExternalInput").ap()
    q_fm = nc.dram_tensor("q_fm", [D, TOK], BF16, kind="ExternalOutput").ap()
    k_fm = nc.dram_tensor("k_fm", [D, TOK], BF16, kind="ExternalOutput").ap()
    v_tm = nc.dram_tensor("v_tm", [TOK, D], BF16, kind="ExternalOutput").ap()
    qp_fm = nc.dram_tensor("qp_fm", [D, TOK], BF16, kind="ExternalOutput").ap()
    kp_fm = nc.dram_tensor("kp_fm", [D, TOK], BF16, kind="ExternalOutput").ap()
    pg = Prog(nc)
    emit_l0_proj(pg, x, wq, wk, wv, wqi, wki, q_fm, k_fm, v_tm, qp_fm, kp_fm)
    pg.barrier()
    pg.finalize()
    return nc


def build_p3_l0rel():
    nc = bass.Bass("TRN2", target_bir_lowering=False)
    qp_fm = nc.dram_tensor("qp_fm", [D, TOK], BF16, kind="ExternalInput").ap()
    kp_all = nc.dram_tensor("kp_all", [4 * D, TOK], BF16, kind="ExternalInput").ap()
    hw = nc.dram_tensor("hw", [1, 8], F32, kind="ExternalInput").ap()
    temp = nc.dram_tensor("temp", [1, 1], F32, kind="ExternalInput").ap()
    rel = nc.dram_tensor("rel", [TOK, 1], F32, kind="ExternalOutput").ap()
    pg = Prog(nc)
    emit_l0_rel(pg, qp_fm, kp_all, hw, temp, rel)
    pg.barrier()
    pg.finalize()
    return nc



def shard_rows(x_full):
    out = []
    for c in range(NCORE):
        b, g = c // 4, c % 4
        rows = []
        for bi in BLOCKS[g]:
            rows.append(x_full[b, bi * 512:(bi + 1) * 512])
        out.append(np.ascontiguousarray(np.concatenate(rows, axis=0)))
    return out


def unshard_rows(shards, dtype=np.float32):
    Dd = shards[0].shape[1]
    x = np.empty((B, S, Dd), dtype=dtype)
    for c in range(NCORE):
        b, g = c // 4, c % 4
        for k, bi in enumerate(BLOCKS[g]):
            x[b, bi * 512:(bi + 1) * 512] = shards[c][k * 512:(k + 1) * 512]
    return x


_prog_cache = {}


def _get_prog(name, builder):
    if name not in _prog_cache:
        _prog_cache[name] = builder()
    return _prog_cache[name]


def run_ffn0(x_full, factors, fbias):
    nc = _get_prog("p1", build_p1_ffn0)
    shards = shard_rows(np.asarray(x_full, dtype=np.float32))
    fac = np.ascontiguousarray(np.asarray(factors, dtype=np.float32))
    fb = np.ascontiguousarray(np.asarray(fbias, dtype=np.float32))
    in_maps = [{"x": shards[c], "factors": fac, "fbias": fb.reshape(1, -1)} for c in range(NCORE)]
    res = run_bass_kernel_spmd(nc, in_maps, core_ids=list(range(NCORE)))
    return unshard_rows([res.results[c]["xo"] for c in range(NCORE)])


def _bf(x):
    return np.ascontiguousarray(np.asarray(x))


def run_l0_proj(x_full, wq, wk, wv, wqi, wki):
    nc = _get_prog("p2", build_p2_l0proj)
    shards = shard_rows(np.asarray(x_full, dtype=np.float32))
    in_maps = [{"x": shards[c], "wq": _bf(wq), "wk": _bf(wk), "wv": _bf(wv),
                "wqi": _bf(wqi), "wki": _bf(wki)} for c in range(NCORE)]
    res = run_bass_kernel_spmd(nc, in_maps, core_ids=list(range(NCORE)))
    return [res.results[c] for c in range(NCORE)]


def gather_batch(per_core_arrays):
    return [np.concatenate([per_core_arrays[b * 4 + r] for r in range(4)], axis=0)
            for b in range(B)]


def run_l0_rel(qp_shards, kp_batch, head_weights, temp):
    nc = _get_prog("p3", build_p3_l0rel)
    in_maps = []
    for c in range(NCORE):
        b = c // 4
        in_maps.append({
            "qp_fm": qp_shards[c],
            "kp_all": kp_batch[b],
            "hw": np.asarray(head_weights, np.float32).reshape(1, 8),
            "temp": np.asarray(temp, np.float32).reshape(1, 1),
        })
    res = run_bass_kernel_spmd(nc, in_maps, core_ids=list(range(NCORE)))
    return [res.results[c]["rel"] for c in range(NCORE)]


def run_l0_attn(x1_full, proj_shards, rels, wo):
    nc = _get_prog("p4", build_p4_l0attn)
    x_shards = shard_rows(np.asarray(x1_full, dtype=np.float32))
    k_batch = gather_batch([np.asarray(s["k_fm"]) for s in proj_shards])
    v_batch = gather_batch([np.asarray(s["v_tm"]) for s in proj_shards])
    rel_batch = gather_batch([np.asarray(r) for r in rels])
    masks = {g: host_masks(g) for g in range(4)}
    in_maps = []
    for c in range(NCORE):
        b, g = c // 4, c % 4
        in_maps.append({
            "q_fm": np.asarray(proj_shards[c]["q_fm"]),
            "k_all": k_batch[b], "v_all": v_batch[b], "rel_all": rel_batch[b],
            "wo": _bf(wo), "maskA": masks[g][0], "maskC": masks[g][1],
            "x": x_shards[c],
        })
    res = run_bass_kernel_spmd(nc, in_maps, core_ids=list(range(NCORE)))
    return unshard_rows([res.results[c]["xo"] for c in range(NCORE)])


def gate_reshape(gk):
    return np.ascontiguousarray(
        np.asarray(gk, np.float32).reshape(16, 128, 8).transpose(1, 0, 2).reshape(128, 128))


def run_moe(x_full, gate_kernel, wi0, wi1, wo, debug=False):
    if debug:
        nc = _get_prog("p6dbg", lambda: build_p6_moe(True))
    else:
        nc = _get_prog("p6", build_p6_moe)
    shards = shard_rows(np.asarray(x_full, dtype=np.float32))
    gr = gate_reshape(gate_kernel)
    in_maps = [{"x": shards[c], "gater": gr, "wi0": _bf(wi0), "wi1": _bf(wi1),
                "wo": _bf(wo)} for c in range(NCORE)]
    res = run_bass_kernel_spmd(nc, in_maps, core_ids=list(range(NCORE)))
    out = unshard_rows([res.results[c]["xo"] for c in range(NCORE)])
    if debug:
        return out, res
    return out


def b_qkv_cols(b_qkv):
    return np.ascontiguousarray(np.asarray(b_qkv, np.float32).reshape(48, 128).T)


def run_l1_proj(x_full, w_qkv, b_qkv):
    nc = _get_prog("p7", build_p7_l1proj)
    shards = shard_rows(np.asarray(x_full, dtype=np.float32))
    wqkvT = np.ascontiguousarray(np.asarray(w_qkv, np.float32).T)
    bc = b_qkv_cols(b_qkv)
    bv = np.ascontiguousarray(np.asarray(b_qkv, np.float32)[4096:].reshape(1, -1))
    in_maps = [{"x": shards[c], "wqkvT": wqkvT, "bcols": bc, "bvrow": bv}
               for c in range(NCORE)]
    res = run_bass_kernel_spmd(nc, in_maps, core_ids=list(range(NCORE)))
    return [res.results[c] for c in range(NCORE)]


def run_l1_attn(x_full, proj_shards, w_out, b_out):
    nc = _get_prog("p8", build_p8_l1attn)
    x_shards = shard_rows(np.asarray(x_full, dtype=np.float32))
    k_batch = gather_batch([np.asarray(s["k_fm"]) for s in proj_shards])
    v_batch = gather_batch([np.asarray(s["v_tm"]) for s in proj_shards])
    woT = np.ascontiguousarray(np.asarray(w_out, np.float32).T)
    bout = np.ascontiguousarray(np.asarray(b_out, np.float32).reshape(1, -1))
    in_maps = []
    for c in range(NCORE):
        b = c // 4
        in_maps.append({
            "q_fm": np.asarray(proj_shards[c]["q_fm"]),
            "k_all": k_batch[b], "v_all": v_batch[b],
            "woT": woT, "bout": bout, "x": x_shards[c],
        })
    res = run_bass_kernel_spmd(nc, in_maps, core_ids=list(range(NCORE)))
    return unshard_rows([res.results[c]["xo"] for c in range(NCORE)])


def kernel(**inputs):
    x = np.asarray(inputs["L0_attn_hidden_states"], np.float32)
    factors = inputs["L0_ffn_factors"]
    fbias = inputs["L0_ffn_bias"]

    x1 = run_ffn0(x, factors, fbias)
    proj = run_l0_proj(x1, inputs["L0_attn_Wq"], inputs["L0_attn_Wk"],
                       inputs["L0_attn_Wv"], inputs["L0_attn_Wq_ind"],
                       inputs["L0_attn_Wk_ind"])
    kp_batch = gather_batch([np.asarray(s["kp_fm"]) for s in proj])
    qp_shards = [np.asarray(s["qp_fm"]) for s in proj]
    rels = run_l0_rel(qp_shards, kp_batch, inputs["L0_attn_head_weights"],
                      inputs["L0_attn_temperature_param"])
    x2 = run_l0_attn(x1, proj, rels, inputs["L0_attn_Wo"])
    x3 = run_ffn0(x2, factors, fbias)

    x4 = run_moe(x3, inputs["L1_ffn_gate_kernel"], inputs["L1_ffn_wi_0"],
                 inputs["L1_ffn_wi_1"], inputs["L1_ffn_wo"])
    proj2 = run_l1_proj(x4, inputs["L1_attn_w_qkv"], inputs["L1_attn_b_qkv"])
    x5 = run_l1_attn(x4, proj2, inputs["L1_attn_w_out"], inputs["L1_attn_b_out"])
    x6 = run_moe(x5, inputs["L1_ffn_gate_kernel"], inputs["L1_ffn_wi_0"],
                 inputs["L1_ffn_wi_1"], inputs["L1_ffn_wo"])
    return np.ascontiguousarray(x6.astype(np.float32))
